# version 04
# baseline (speedup 1.0000x reference)
import contextlib
import numpy as np
import ml_dtypes

import concourse.bass as bass
import concourse.mybir as mybir
from concourse.bass_utils import run_bass_kernel_spmd

F32 = mybir.dt.float32
BF16 = mybir.dt.bfloat16
U16 = mybir.dt.uint16
AF = mybir.ActivationFunctionType
ALU = mybir.AluOpType
BF = ml_dtypes.bfloat16

B, T, D = 8, 4096, 2048
NST, P = 8, 512
DC, NN, IC = 16, 16, 64
INTER, DT_RANK = 8192, 128
RS2048 = float(1.0 / np.sqrt(2048.0))
RS8192 = float(1.0 / np.sqrt(8192.0))
DN_GROUPS = [(0, 6), (6, 6), (12, 4)]

ENGINES = ("pe", "act", "dve", "pool", "sync")


def _had(n):
    i = np.arange(n)
    pc = np.bitwise_count(i[:, None] & i[None, :])
    return np.where(pc % 2 == 0, 1.0, -1.0).astype(np.float32)


class Em:

    def __init__(self):
        self.streams = {e: [] for e in ENGINES}
        self.semnames = []
        self.counts = {}
        self.waited = {e: {} for e in ENGINES}

    def sem(self, name):
        self.semnames.append(name)
        self.counts[name] = 0

    def emit(self, eng, fn, deps=(), inc=None, inc_by=1):
        waits = []
        for d in deps:
            if d is None:
                continue
            s, v, peng = d
            if peng == eng and eng != "sync":
                continue
            if self.waited[eng].get(s, 0) >= v:
                continue
            self.waited[eng][s] = v
            waits.append((s, v))
        tok = None
        if inc is not None:
            self.counts[inc] += inc_by
            tok = (inc, self.counts[inc], eng)
        self.streams[eng].append((waits, fn, inc, inc_by))
        return tok

    def dma(self, eng, fn, inc, deps=()):
        return self.emit(eng, fn, deps=deps, inc=inc, inc_by=16)


def build_program(nst=NST, with_fm=True, with_dn=True, with_fold=True, with_cast=True, debug_tap=False):
    nc = bass.Bass("TRN2", target_bir_lowering=False)

    d_x = nc.dram_tensor("x_sw", [NST * DC * P * 2, 128], U16, kind="ExternalInput").ap()
    d_alog = nc.dram_tensor("a_log_r", [128, DC * NN], F32, kind="ExternalInput").ap()
    d_D = nc.dram_tensor("d_r", [128, DC], F32, kind="ExternalInput").ap()
    d_dtb = nc.dram_tensor("dtb_r", [128, DC], F32, kind="ExternalInput").ap()
    d_wdt = nc.dram_tensor("wdt_s", [128, D], F32, kind="ExternalInput").ap()
    d_xpw = nc.dram_tensor("xpw_s", [128, DC * 160], F32, kind="ExternalInput").ap()
    d_wscu = nc.dram_tensor("wsc_up_r", [128, IC], F32, kind="ExternalInput").ap()
    d_wscg = nc.dram_tensor("wsc_g_r", [128, IC], F32, kind="ExternalInput").ap()
    d_sudp = nc.dram_tensor("su_dp_r", [128, IC], F32, kind="ExternalInput").ap()
    d_wscd = nc.dram_tensor("wsc_dp_r", [128, DC], F32, kind="ExternalInput").ap()
    d_suug = nc.dram_tensor("su_ug_r", [128, DC], F32, kind="ExternalInput").ap()
    d_wup = nc.dram_tensor("wup_s", [IC, 128, D], F32, kind="ExternalInput").ap()
    d_wg = nc.dram_tensor("wg_s", [IC, 128, D], F32, kind="ExternalInput").ap()
    d_wdp1 = nc.dram_tensor("wd_p1", [128, 64 * D], F32, kind="ExternalInput").ap()
    d_h128 = nc.dram_tensor("h128", [128, 128], BF16, kind="ExternalInput").ap()
    d_h64 = nc.dram_tensor("h64", [64, 64], BF16, kind="ExternalInput").ap()
    d_ident = nc.dram_tensor("ident", [128, 128], BF16, kind="ExternalInput").ap()
    d_ones = nc.dram_tensor("ones_bf", [128, 1], BF16, kind="ExternalInput").ap()

    d_out = nc.dram_tensor("y", [128, DC, T], F32, kind="ExternalOutput").ap()
    d_wug = nc.dram_tensor("wug_bf", [IC, 2, 128, D], BF16, kind="Internal").ap()
    d_s1 = nc.dram_tensor("fold_s1", [64, 128, D], BF16, kind="Internal").ap()
    d_wd2 = nc.dram_tensor("wd2", [64, 128, D], BF16, kind="Internal").ap()
    d_a = nc.dram_tensor("a_str", [IC, 128, P], BF16, kind="Internal").ap()
    d_us = nc.dram_tensor("us_str", [IC, 128, P], BF16, kind="Internal").ap()
    d_bc = nc.dram_tensor("bc_bnc", [2, 32, P], BF16, kind="Internal").ap()
    d_st = nc.dram_tensor("st_bnc", [2, 2, P], F32, kind="Internal").ap()
    d_dbg = nc.dram_tensor("dbg_tap", [4, 128, DC, P], BF16, kind="Internal").ap()
    d_dbg2 = nc.dram_tensor("dbg_tap2", [4, 128, P], F32, kind="Internal").ap()

    em = Em()
    for n in ("ld", "gld", "pe_a", "pe_up", "pe_g", "pe_id", "pe_dn", "pe_f",
              "act_m", "act_e", "act_g", "dve", "dve_s", "pool", "outd"):
        em.sem(n)

    with contextlib.ExitStack() as stk:
        sb = lambda name, shape, dt: stk.enter_context(nc.sbuf_tensor(name, shape, dt))
        pst = lambda name: stk.enter_context(nc.psum_tensor(name, [128, P], F32))

        c_negA = sb("negA", [128, DC * NN], F32)
        c_D = sb("cD", [128, DC], F32)
        c_dtb = sb("cdtb", [128, DC], F32)
        c_wdt = sb("cwdt", [128, D], BF16)
        c_xpw = sb("cxpw", [128, DC * 160], BF16)
        c_sup = sb("csup", [128, IC], F32)
        c_sg = sb("csg", [128, IC], F32)
        c_sdp = sb("csdp", [128, DC], F32)
        c_suug = sb("csuug", [128, DC], F32)
        c_h128 = sb("ch128", [128, 128], BF16)
        c_h64 = sb("ch64", [64, 64], BF16)
        c_id = sb("cid", [128, 128], BF16)
        c_ones = sb("cones", [128, 1], BF16)
        c_carry = sb("carry", [128, DC * NN], F32)
        c_eps = sb("ceps", [128, 1], F32)

        t_xtr = sb("xtr", [128, 4 * P], F32)
        t_xbf = sb("xbf", [128, DC, P], BF16)
        t_xn = sb("xn", [128, DC, P], BF16)
        t_dl = sb("dl", [128, DC, P], BF16)
        t_du = sb("du", [128, DC, P], BF16)
        t_bc = sb("bcb", [128, 2 * NN, P], BF16)
        t_ym = sb("ym", [128, DC, P], BF16)
        t_dr = sb("dr", [128, P], BF16)
        t_bcsb = sb("bcsb", [32, P], BF16)
        t_sq = sb("sq", [128, 2 * P], BF16)
        t_mrb = sb("mrb", [128, P], F32)
        t_rsb = sb("rsb", [128, P], F32)

        t_dA = sb("dA", [128, 2 * P], F32)
        t_dBu = sb("dBu", [128, 2 * P], BF16)
        t_h = sb("th", [128, 4 * P], BF16)
        t_hC = sb("thC", [128, NN * P], BF16)
        t_us = sb("us", [128, 2 * P], BF16)
        t_gs = sb("gs", [128, 2 * P], BF16)
        t_ab = sb("ab", [128, 2 * P], BF16)
        t_ain = sb("ain", [128, 2, 4, P], BF16)
        c_zero = sb("czero", [128, 128], BF16)
        t_wug = sb("wug", [128, 2, 2, D], BF16)
        t_wdn = sb("wdn", [128, 6, 768], BF16)
        t_out = t_hC[:, 12 * P:16 * P].bitcast(F32)
        t_cast = t_bc[:].rearrange("p n t -> p (n t)")[:, 0:8192].rearrange(
            "p (a f) -> p a f", a=2)
        _dlf = t_dl[:].rearrange("p c t -> p (c t)")
        t_f1 = _dlf[:, 0:4 * P]
        t_f1e = _dlf[:, 4 * P:8 * P]
        _duf = t_du[:].rearrange("p c t -> p (c t)")
        t_f2 = _duf[0:64, 0:6 * P]
        t_f2e = _duf[0:64, 6 * P:12 * P]
        t_btmp = t_sq[:, 0:P]
        _xtf = t_xtr[:].rearrange("p (a t) -> p a t", a=4)
        t_sr = {"mean": _xtf[0:1, 0, :], "ex2": _xtf[0:1, 1, :],
                "m2": _xtf[0:1, 2, :], "var": _xtf[0:1, 3, :],
                "lnv": t_dA[0:1, 0:P], "rstd": t_rsb[0:1, :], "mr": t_mrb[0:1, :]}

        ps_u = [pst("ps_u0"), pst("ps_u1")]
        ps_g = [pst("ps_g0"), pst("ps_g1")]
        ps_ym = [pst("ps_ym0"), pst("ps_ym1")]
        ps_a = pst("ps_a")
        ps_m = pst("ps_m")
        ps_dn = ps_u + ps_g + ps_ym

        ld0 = []
        for dst, src in ((c_negA, d_alog), (c_D, d_D), (c_dtb, d_dtb),
                         (c_sup, d_wscu), (c_sg, d_wscg), (c_sdp, d_wscd),
                         (c_suug, d_suug), (c_h128, d_h128), (c_h64, d_h64),
                         (c_id, d_ident), (c_ones, d_ones)):
            ld0.append(em.dma("sync", (lambda a=dst, b=src: lambda e: e.dma_start(out=a[:], in_=b))(), inc="ld"))
        ld0.append(em.dma("sync", lambda e: e.dma_start(out=t_mrb[:, 0:IC], in_=d_sudp), inc="ld"))
        ld0.append(em.dma("pool", lambda e: e.dma_start(out=c_wdt[:], in_=d_wdt), inc="gld"))
        ld0.append(em.dma("pool", lambda e: e.dma_start(out=c_xpw[:], in_=d_xpw), inc="gld"))

        t_sqz = em.emit("dve", lambda e: e.memset(t_sq[:], 0.0), inc="dve")
        for i in range(4):
            em.emit("act", lambda e: e.activation(out=t_sq[:, 0:8], in_=t_sq[:, 0:8], func=AF.Copy),
                    deps=[t_sqz])
        em.emit("act", lambda e: e.activation(out=t_sq[:, 0:8], in_=t_sq[:, 0:8], func=AF.Exp))
        em.emit("act", lambda e: e.activation(out=t_sq[:, 0:8], in_=t_sq[:, 0:8], func=AF.Silu))
        em.emit("act", lambda e: e.activation(out=t_sq[:, 0:8], in_=t_sq[:, 0:8], func=AF.Exp))
        tA = em.emit("act", lambda e: e.activation(out=c_negA[:], in_=c_negA[:], func=AF.Exp),
                     deps=ld0, inc="act_e")
        em.emit("dve", lambda e: e.tensor_scalar_mul(out=c_negA[:], in0=c_negA[:], scalar1=-1.0),
                deps=[tA], inc="dve")
        em.emit("dve", lambda e: e.tensor_mul(out=c_sup[:], in0=c_sup[:], in1=t_mrb[:, 0:IC]),
                deps=ld0, inc="dve")
        em.emit("dve", lambda e: e.tensor_scalar_mul(out=c_sup[:], in0=c_sup[:], scalar1=RS2048), inc="dve")
        em.emit("dve", lambda e: e.tensor_scalar_mul(out=c_sg[:], in0=c_sg[:], scalar1=RS2048),
                deps=ld0, inc="dve")
        em.emit("dve", lambda e: e.tensor_scalar_mul(out=c_sdp[:], in0=c_sdp[:], scalar1=RS8192),
                deps=ld0, inc="dve")
        t_cz = em.emit("dve", lambda e: e.memset(c_carry[:], 0.0), inc="dve")
        t_eps = em.emit("dve", lambda e: e.memset(c_eps[:], 1e-5), inc="dve")
        t_z0 = em.emit("dve", lambda e: e.memset(c_zero[:], 0.0), inc="dve")

        if not with_fold:
            f1_mm = [em.emit("pe", lambda e: e.matmul(ps_a[:, 0:128], c_h128[:], c_id[:], start=True, stop=True),
                             deps=ld0, inc="pe_f")]
            f2_mm = f1_mm
            f2_ev = [em.emit("act", lambda e: e.activation(out=t_sq[:, 0:8], in_=t_sq[:, 0:8], func=AF.Copy),
                             deps=[f1_mm[0]], inc="act_e")]
            f2_out = [em.dma("sync", lambda e: e.dma_start(out=d_s1[0, :, 0:8], in_=t_sq[:, 0:8]),
                             inc="ld", deps=[f2_ev[0]])]
            fold_done = f2_out[-1]
        if with_fold:
            f1_mm, f1_ev, f1_out, f1_in = [], [], [], []
            base1 = em.counts["act_e"]
            for i in range(256):
                b = i % 4
                c1, os = i // 4, i % 4
                deps = [ld0[-1]]
                if i >= 4:
                    deps.append(f1_mm[i - 4])
                f1_in.append(em.dma("pool",
                    (lambda b=b, i=i: lambda e: e.dma_start(
                        out=t_f1[:, b * P:(b + 1) * P], in_=d_wdp1[:, i * P:(i + 1) * P]))(),
                    inc="gld", deps=deps))
                deps = [ld0[8], f1_in[i]]
                if i >= 4:
                    deps.append(("act_e", base1 + i - 3, "act"))
                f1_mm.append(em.emit("pe",
                    (lambda b=b: lambda e: e.matmul(ps_dn[b][:], c_h128[:], t_f1[:, b * P:(b + 1) * P],
                                                    start=True, stop=True))(),
                    deps=deps, inc="pe_f"))
            for w in range(4):
                f1_mm.append(em.emit("pe", lambda e: e.matmul(ps_a[:], c_h128[:], t_f1[:, 0:P],
                                                              start=True, stop=True), inc="pe_f"))
            for i in range(256):
                b = i % 4
                c1, os = i // 4, i % 4
                deps = [f1_mm[min(i + 3, 259)]]
                if i >= 4:
                    deps.append(f1_out[i - 4])
                f1_ev.append(em.emit("act",
                    (lambda b=b: lambda e: e.activation(out=t_f1e[:, b * P:(b + 1) * P], in_=ps_dn[b][:],
                                                        func=AF.Copy))(),
                    deps=deps, inc="act_e"))
                f1_out.append(em.dma("sync",
                    (lambda b=b, c1=c1, os=os: lambda e: e.dma_start(
                        out=d_s1[c1, :, os * P:(os + 1) * P], in_=t_f1e[:, b * P:(b + 1) * P]))(),
                    inc="ld", deps=[("act_e", min(base1 + i + 2, base1 + 256), "act")]))

            s1v = d_s1.rearrange("c p o -> c (p o)")
            f2_mm, f2_ev, f2_out, f2_in = [], [], [], []
            base2 = em.counts["act_e"]

            def emit_f2_out(q):
                b = q % 6
                pp, os = q // 4, q % 4
                f2_out.append(em.dma("sync",
                    (lambda b=b, pp=pp, os=os: lambda e: e.dma_start(
                        out=d_wd2[:, pp, os * P:(os + 1) * P], in_=t_f2e[:, b * P:(b + 1) * P]))(),
                    inc="ld", deps=[("act_e", min(base2 + q + 2, base2 + 512), "act")]))

            for q in range(512):
                b = q % 6
                deps = [f1_out[255]]
                if q >= 6:
                    deps.append(f2_mm[q - 6])
                f2_in.append(em.dma("sync",
                    (lambda b=b, q=q: lambda e: e.dma_start(
                        out=t_f2[:, b * P:(b + 1) * P], in_=s1v[:, q * P:(q + 1) * P]))(),
                    inc="ld", deps=deps))
                deps = [f2_in[q]]
                if q >= 6:
                    deps.append(("act_e", base2 + q - 5, "act"))
                f2_mm.append(em.emit("pe",
                    (lambda b=b: lambda e: e.matmul(ps_dn[b % 6][0:64, :], c_h64[:], t_f2[:, b * P:(b + 1) * P],
                                                    start=True, stop=True))(),
                    deps=deps, inc="pe_f"))
                if q >= 6:
                    emit_f2_out(q - 6)
            for w in range(4):
                f2_mm.append(em.emit("pe", lambda e: e.matmul(ps_a[0:64, :], c_h64[:], t_f2[:, 0:P],
                                                              start=True, stop=True), inc="pe_f"))
            for q in range(506, 512):
                emit_f2_out(q)
            for q in range(512):
                b = q % 6
                deps = [f2_mm[min(q + 5, 515)]]
                if q >= 6:
                    deps.append(f2_out[q - 6])
                f2_ev.append(em.emit("act",
                    (lambda b=b: lambda e: e.activation(out=t_f2e[:, b * P:(b + 1) * P], in_=ps_dn[b % 6][0:64, :],
                                                        func=AF.Copy))(),
                    deps=deps, inc="act_e"))
            fold_done = f2_out[-1]


        cast_done = {}
        prev_out = [None, None]
        for ic in (range(IC) if with_cast else []):
            b = ic % 2
            deps = [prev_out[b]] if prev_out[b] else []
            i1 = em.dma("pool",
                (lambda b=b, ic=ic: lambda e: e.dma_start(out=t_cast[:, b, 0:D], in_=d_wup[ic]))(),
                inc="gld", deps=deps)
            i2 = em.dma("pool",
                (lambda b=b, ic=ic: lambda e: e.dma_start(out=t_cast[:, b, D:2 * D], in_=d_wg[ic]))(),
                inc="gld")
            em.dma("sync",
                (lambda b=b, ic=ic: lambda e: e.dma_start(out=d_wug[ic, 0], in_=t_cast[:, b, 0:D]))(),
                inc="ld", deps=[i1])
            cast_done[ic] = em.dma("sync",
                (lambda b=b, ic=ic: lambda e: e.dma_start(out=d_wug[ic, 1], in_=t_cast[:, b, D:2 * D]))(),
                inc="ld", deps=[i2])
            prev_out[b] = cast_done[ic]

        X = {
            "last_A_pe": f2_mm[-1],
            "gate_last": None, "g_last2": None,
            "up_last": None, "up_last2": None,
            "us_out_last": None, "us_in_last2": None,
            "dn_out_last": None,
            "idx_last": None,
            "hC_prev": {},
            "dBu_last": None,
            "dn_stt_last": f2_ev[-1],
            "dn_mm_prev_last": None,
            "carry_prev_last": None,
            "id_prev_last": None,
            "up_ev_last": None, "g_ev_last": None,
            "silu_last": None,
            "a_out_prev_last": None,
            "out_prev": {},
            "scan_last2": {},
            "xh_prev_done": None,
            "setup_cast_last": cast_done.get(IC - 1),
        }

        def phase_A(j):
            tk = {}
            sum_mm, xbf_cast, sq_act = [], [], []
            for quarter in range(4):
                rows0 = (j * DC + quarter * 4) * P * 2
                if quarter == 0:
                    dep = [X["last_A_pe"], X["idx_last"]]
                else:
                    dep = [xbf_cast[4 * quarter - 1]]
                q_tok = em.dma("sync",
                    (lambda rows0=rows0: lambda e: e.dma_start(
                        out=t_xtr[:].bitcast(U16), in_=d_x[rows0:rows0 + 4 * P * 2, :], transpose=True))(),
                    inc="ld", deps=dep)
                for dcl in range(4):
                    dc = quarter * 4 + dcl
                    xbf_cast.append(em.emit("dve",
                        (lambda dc=dc, dcl=dcl: lambda e: e.tensor_copy(
                            out=t_xbf[:, dc, :], in_=t_xtr[:, dcl * P:(dcl + 1) * P]))(),
                        deps=[q_tok, X["idx_last"]], inc="dve"))
            for dc in range(DC):
                deps = [xbf_cast[dc]]
                if dc >= 2:
                    deps.append(sum_mm[2 * (dc - 2) + 1])
                sq_act.append(em.emit("act",
                    (lambda dc=dc: lambda e: e.activation(
                        out=t_sq[:, (dc % 2) * P:(dc % 2 + 1) * P], in_=t_xbf[:, dc, :], func=AF.Square))(),
                    deps=deps, inc="act_e"))
                sum_mm.append(em.emit("pe",
                    (lambda dc=dc: lambda e: e.matmul(ps_a[0:1, :], c_ones[:], t_xbf[:, dc, :],
                                                      start=(dc == 0), stop=(dc == DC - 1)))(),
                    deps=[xbf_cast[dc]] + ([X["xh_prev_done"]] if dc == 0 else []), inc="pe_a"))
                sum_mm.append(em.emit("pe",
                    (lambda dc=dc: lambda e: e.matmul(ps_m[0:1, :], c_ones[:],
                                                      t_sq[:, (dc % 2) * P:(dc % 2 + 1) * P],
                                                      start=(dc == 0), stop=(dc == DC - 1)))(),
                    deps=[sq_act[dc]], inc="pe_a"))
            st1 = em.emit("dve", lambda e: e.tensor_scalar_mul(out=t_sr["mean"][:], in0=ps_a[0:1, :],
                                                               scalar1=1.0 / 2048),
                          deps=[sum_mm[-2], sum_mm[-1]], inc="dve")
            em.emit("dve", lambda e: e.tensor_scalar_mul(out=t_sr["ex2"][:], in0=ps_m[0:1, :],
                                                         scalar1=1.0 / 2048), inc="dve")
            em.emit("dve", lambda e: e.tensor_mul(out=t_sr["m2"][:], in0=t_sr["mean"][:], in1=t_sr["mean"][:]),
                    inc="dve")
            st4 = em.emit("dve", lambda e: e.tensor_sub(out=t_sr["var"][:], in0=t_sr["ex2"][:], in1=t_sr["m2"][:]),
                          inc="dve")
            em.emit("act", lambda e: e.activation(out=t_sr["lnv"][:], in_=t_sr["var"][:], func=AF.Ln,
                                                   bias=c_eps[0:1]),
                    deps=[st4, t_eps], inc="act_e")
            st6 = em.emit("act", lambda e: e.activation(out=t_sr["rstd"][:], in_=t_sr["lnv"][:], func=AF.Exp,
                                                        scale=-0.5), inc="act_e")
            st7 = em.emit("dve", lambda e: e.tensor_mul(out=t_sr["mr"][:], in0=t_sr["mean"][:], in1=t_sr["rstd"][:]),
                          deps=[st6], inc="dve")
            st7 = em.emit("dve", lambda e: e.tensor_copy(out=c_eps[:], in_=c_eps[:]), inc="dve")
            so1 = em.dma("sync", (lambda jj=j: lambda e: e.dma_start(out=d_st[jj % 2, 0], in_=t_sr["mr"][:]))(),
                         inc="ld", deps=[st7])
            so2 = em.dma("sync", (lambda jj=j: lambda e: e.dma_start(out=d_st[jj % 2, 1], in_=t_sr["rstd"][:]))(),
                         inc="ld", deps=[st7])
            bi1 = em.dma("sync", (lambda jj=j: lambda e: e.dma_start(
                out=t_mrb[:], in_=d_st[jj % 2, 0:1].partition_broadcast(128)))(), inc="ld", deps=[so1])
            bi2 = em.dma("sync", (lambda jj=j: lambda e: e.dma_start(
                out=t_rsb[:], in_=d_st[jj % 2, 1:2].partition_broadcast(128)))(), inc="ld", deps=[so2])
            ln_ap = []
            for dc in range(DC):
                em.emit("dve",
                    (lambda dc=dc: lambda e: e.tensor_mul(out=t_xn[:, dc, :], in0=t_xbf[:, dc, :], in1=t_rsb[:]))(),
                    deps=[bi1, bi2, X["gate_last"]], inc="dve")
                ln_ap.append(em.emit("dve",
                    (lambda dc=dc: lambda e: e.tensor_sub(out=t_xn[:, dc, :], in0=t_xn[:, dc, :], in1=t_mrb[:]))(),
                    inc="dve"))
            xp_mm = []
            for kc in range(DC):
                xp_mm.append(em.emit("pe",
                    (lambda kc=kc: lambda e: e.matmul(ps_a[:], c_xpw[:, kc * 160:kc * 160 + 128],
                                                      t_xn[:, kc, :], start=(kc == 0), stop=(kc == DC - 1)))(),
                    deps=[ln_ap[kc]], inc="pe_a"))
                xp_mm.append(em.emit("pe",
                    (lambda kc=kc: lambda e: e.matmul(ps_m[0:32, :], c_xpw[:, kc * 160 + 128:(kc + 1) * 160],
                                                      t_xn[:, kc, :], start=(kc == 0), stop=(kc == DC - 1)))(),
                    inc="pe_a"))
            xproj = xp_mm[-1]
            dr_ev = em.emit("act", lambda e: e.activation(out=t_dr[:], in_=ps_a[:], func=AF.Copy),
                            deps=[xproj], inc="act_e")
            bc_ev = em.emit("act", lambda e: e.activation(out=t_bcsb[:], in_=ps_m[0:32, :], func=AF.Copy),
                            deps=[xproj], inc="act_e")
            bco = em.dma("sync", (lambda jj=j: lambda e: e.dma_start(out=d_bc[jj % 2], in_=t_bcsb[:]))(),
                         inc="ld", deps=[bc_ev, ("act_e", bc_ev[1] + 1, "act")])
            bc_in = []
            for n in range(2 * NN):
                deps = [bco]
                if j == 0:
                    deps.append(X["setup_cast_last"])
                if j > 0:
                    deps.append(X["hC_prev"].get((DC - 1, n % NN)))
                    if n < NN:
                        deps.append(X["dBu_last"])
                bc_in.append(em.dma("sync",
                    (lambda jj=j, n=n: lambda e: e.dma_start(
                        out=t_bc[:, n, :], in_=d_bc[jj % 2, n:n + 1].partition_broadcast(128)))(),
                    inc="ld", deps=deps))
            tk["bc_in"] = bc_in
            dl_ev, dt_mm, exp_t = [], [], []

            def emit_sp(dc):
                bank = ps_a if dc % 2 == 0 else ps_m
                exp_t.append(em.emit("act",
                    (lambda dc=dc, bank=bank: lambda e: e.activation(
                        out=t_sq[:, (dc % 2) * P:(dc % 2 + 1) * P], in_=bank[:], func=AF.Exp,
                        bias=c_dtb[:, dc:dc + 1]))(),
                    deps=[dt_mm[min(dc + 1, DC - 1)]], inc="act_e"))
                dl_ev.append(em.emit("act",
                    (lambda dc=dc: lambda e: e.activation(
                        out=t_dl[:, dc, :], in_=t_sq[:, (dc % 2) * P:(dc % 2 + 1) * P], func=AF.Ln,
                        bias=1.0))(),
                    inc="act_e"))

            for dc in range(DC):
                bank = ps_a if dc % 2 == 0 else ps_m
                deps = [dr_ev, bc_ev]
                if dc >= 2:
                    deps.append(exp_t[dc - 2])
                dt_mm.append(em.emit("pe",
                    (lambda dc=dc, bank=bank: lambda e: e.matmul(bank[:], c_wdt[:, dc * 128:(dc + 1) * 128],
                                                                 t_dr[:], start=True, stop=True))(),
                    deps=deps, inc="pe_a"))
                if dc >= 1:
                    emit_sp(dc - 1)
            emit_sp(DC - 1)
            tk["dl_ev"] = dl_ev
            du_t, df_t = [], []
            for dc in range(DC):
                deps = [dl_ev[dc]]
                if j > 0:
                    deps.append(X["dBu_last"])
                du_t.append(em.emit("dve",
                    (lambda dc=dc: lambda e: e.tensor_mul(out=t_du[:, dc, :], in0=t_dl[:, dc, :],
                                                          in1=t_xn[:, dc, :]))(),
                    deps=deps, inc="dve"))
                df_t.append(em.emit("dve",
                    (lambda dc=dc: lambda e: e.scalar_tensor_tensor(
                        out=t_xbf[:, dc, :], in0=t_xn[:, dc, :], scalar=c_D[:, dc:dc + 1],
                        in1=t_xbf[:, dc, :], op0=ALU.mult, op1=ALU.add))(),
                    inc="dve"))
            tk["du_done"] = du_t[-1]
            tk["dfold"] = df_t
            su_t = []
            for dc in range(DC):
                su_t.append(em.emit("dve",
                    (lambda dc=dc: lambda e: e.tensor_scalar_mul(
                        out=t_xn[:, dc, :], in0=t_xn[:, dc, :], scalar1=c_suug[:, dc:dc + 1]))(),
                    deps=[xproj, du_t[dc], df_t[dc]], inc="dve"))
            bf_last = None
            for s_ in range(4):
                step = 1 << s_
                for base in range(0, DC, 2 * step):
                    for off in range(step):
                        ca, cb2 = base + off, base + off + step
                        d0 = [su_t[ca], su_t[cb2]] if s_ == 0 else []
                        em.emit("pool", (lambda ca=ca: lambda e: e.tensor_copy(
                            out=t_btmp[:], in_=t_xn[:, ca, :]))(), deps=d0, inc="pool")
                        em.emit("pool", (lambda ca=ca, cb2=cb2: lambda e: e.tensor_add(
                            out=t_xn[:, ca, :], in0=t_xn[:, ca, :], in1=t_xn[:, cb2, :]))(), inc="pool")
                        bf_last = em.emit("pool", (lambda cb2=cb2: lambda e: e.tensor_sub(
                            out=t_xn[:, cb2, :], in0=t_btmp[:], in1=t_xn[:, cb2, :]))(), inc="pool")
            xh_ev, h_mm = [], []

            def emit_xh(c2):
                bank = ps_a if c2 % 2 == 0 else ps_m
                xh_ev.append(em.emit("act",
                    (lambda c2=c2, bank=bank: lambda e: e.activation(out=t_xn[:, c2, :], in_=bank[:],
                                                                     func=AF.Copy))(),
                    deps=[h_mm[min(c2 + 1, DC - 1)]], inc="act_e"))

            for c2 in range(DC):
                bank = ps_a if c2 % 2 == 0 else ps_m
                deps = [bf_last]
                if c2 >= 2:
                    deps.append(xh_ev[c2 - 2])
                h_mm.append(em.emit("pe",
                    (lambda c2=c2, bank=bank: lambda e: e.matmul(bank[:], c_h128[:], t_xn[:, c2, :],
                                                                 start=True, stop=True))(),
                    deps=deps, inc="pe_a"))
                if c2 >= 1:
                    emit_xh(c2 - 1)
            emit_xh(DC - 1)
            tk["xh_done"] = xh_ev[-1]
            tk["last_A_pe"] = h_mm[-1]
            return tk

        def phase_fm(j, A):
            up_mm, g_mm = {}, {}
            up_ev, g_ev = {}, {}
            base_g = em.counts["act_g"]
            base_s = em.counts["dve_s"]
            scan_t, hC_t, dBu_t = {}, {}, {}
            id_last = {}
            base_id = em.counts["pe_id"]
            carry_b = {}
            ym_ev = [None] * DC
            silu_t, a_out = {}, {}
            wugu_d, wugg_d = {}, {}
            xident = [None] * DC

            def emit_wug_up(ic):
                deps = []
                if j == 0:
                    deps.append(cast_done[ic])
                if ic >= 2:
                    deps.append(up_mm[ic - 2])
                else:
                    deps.append(X["up_last2"] if ic == 0 else X["up_last"])
                wugu_d[ic] = em.dma("sync",
                    (lambda ic=ic, b=ic % 2: lambda e: e.dma_start(
                        out=t_wug[:, b, 0], in_=d_wug[ic, 0]))(),
                    inc="ld", deps=deps)

            def emit_wug_g(ic):
                deps = []
                if j == 0:
                    deps.append(cast_done[ic])
                if ic >= 2:
                    deps.append(g_mm[ic - 2])
                else:
                    deps.append(X["g_last2"] if ic == 0 else X["gate_last"])
                wugg_d[ic] = em.dma("sync",
                    (lambda ic=ic, b=ic % 2: lambda e: e.dma_start(
                        out=t_wug[:, b, 1], in_=d_wug[ic, 1]))(),
                    inc="ld", deps=deps)

            def emit_up(ic):
                b = ic % 2
                deps = [wugu_d[ic], A["xh_done"]]
                if ic >= 2:
                    deps.append(up_ev[ic - 2])
                else:
                    deps.append(X["up_ev_last"])
                    deps.append(X["dn_stt_last"])
                for kc in range(DC):
                    up_mm[ic] = em.emit("pe",
                        (lambda ic=ic, b=b, kc=kc: lambda e: e.matmul(
                            ps_u[ic % 2][:], t_wug[:, b, 0, kc * 128:(kc + 1) * 128],
                            t_xn[:, kc, :], start=(kc == 0), stop=(kc == DC - 1)))(),
                        deps=(deps if kc == 0 else []), inc=("pe_up" if kc == DC - 1 else None))

            def emit_gate(ic):
                b = ic % 2
                deps = [wugg_d[ic]]
                if ic >= 2:
                    deps.append(("act_g", base_g + ic - 1, "act"))
                else:
                    deps.append(X["g_ev_last"])
                for kc in range(DC):
                    g_mm[ic] = em.emit("pe",
                        (lambda ic=ic, b=b, kc=kc: lambda e: e.matmul(
                            ps_g[ic % 2][:], t_wug[:, b, 1, kc * 128:(kc + 1) * 128],
                            t_xn[:, kc, :], start=(kc == 0), stop=(kc == DC - 1)))(),
                        deps=(deps if kc == 0 else []), inc=("pe_g" if kc == DC - 1 else None))

            us_out = {}

            def emit_upev(ic, extra=None):
                deps = [up_mm[min(ic + 1, IC - 1)]]
                if extra is not None:
                    deps.append(extra)
                if ic >= 2:
                    deps.append(us_out[ic - 2])
                else:
                    deps.append(X["us_out_last"])
                up_ev[ic] = em.emit("act",
                    (lambda ic=ic: lambda e: e.activation(
                        out=t_us[:, (ic % 2) * P:(ic % 2 + 1) * P], in_=ps_u[ic % 2][:], func=AF.Copy,
                        scale=c_sup[:, ic:ic + 1]))(),
                    deps=deps, inc="act_e")
                us_out[ic] = em.dma("sync",
                    (lambda ic=ic: lambda e: e.dma_start(
                        out=d_us[ic], in_=t_us[:, (ic % 2) * P:(ic % 2 + 1) * P]))(),
                    inc="ld", deps=[up_ev[ic], ("act_e", up_ev[ic][1] + 1, "act")])

            def emit_slices(dc):
                for n in range(NN):
                    k = dc * NN + n
                    b2, b4 = k % 2, k % 4
                    deps = [A["dl_ev"][dc]]
                    if k >= 2:
                        deps.append(scan_t[k - 2])
                    else:
                        deps.append(X["scan_last2"].get(k))
                    dA_tok = em.emit("act",
                        (lambda dc=dc, n=n, b2=b2: lambda e: e.activation(
                            out=t_dA[:, b2 * P:(b2 + 1) * P], in_=t_dl[:, dc, :], func=AF.Exp,
                            scale=c_negA[:, dc * NN + n:dc * NN + n + 1]))(),
                        deps=deps, inc="act_m")
                    deps = [A["du_done"], A["bc_in"][n]]
                    if k >= 2:
                        deps.append(scan_t[k - 2])
                    else:
                        deps.append(X["scan_last2"].get(k))
                    dBu_t[k] = em.emit("pool",
                        (lambda dc=dc, n=n, b2=b2: lambda e: e.tensor_tensor(
                            out=t_dBu[:, b2 * P:(b2 + 1) * P], in0=t_du[:, dc, :],
                            in1=t_bc[:, n, :], op=ALU.mult))(),
                        deps=deps, inc="pool")
                    deps = [dA_tok, dBu_t[k], t_cz]
                    if k < DC * NN - 1:
                        deps.append(("pool", dBu_t[k][1] + 1, "pool"))
                    if n % 4 != 3:
                        deps.append(("act_m", dA_tok[1] + 1, "act"))
                    hcov = ((k - 4) // 4) * 4 + 3
                    if k >= 4:
                        deps.append(carry_b[hcov])
                    else:
                        deps.append(X["carry_prev_last"])
                    scan_t[k] = em.emit("dve",
                        (lambda dc=dc, n=n, b2=b2, b4=b4: lambda e: e.tensor_tensor_scan(
                            out=t_h[:, b4 * P:(b4 + 1) * P], data0=t_dA[:, b2 * P:(b2 + 1) * P],
                            data1=t_dBu[:, b2 * P:(b2 + 1) * P],
                            initial=c_carry[:, dc * NN + n:dc * NN + n + 1],
                            op0=ALU.mult, op1=ALU.add))(),
                        deps=deps, inc="dve")
                    deps = [A["bc_in"][NN + n]]
                    if k >= NN:
                        kk = k - NN
                        deps.append(("pe_id", base_id + (kk // NN) * 19 + (kk % NN) + 2, "pe"))
                    else:
                        deps.append(X["id_prev_last"])
                        deps.append(X["dn_out_last"])
                    hC_t[k] = em.emit("dve",
                        (lambda n=n, b4=b4, b16=k % NN: lambda e: e.tensor_tensor(
                            out=t_hC[:, b16 * P:(b16 + 1) * P], in0=t_h[:, b4 * P:(b4 + 1) * P],
                            in1=t_bc[:, NN + n, :], op=ALU.mult))(),
                        deps=deps, inc="dve")
                    if n % 4 == 3:
                        hview = t_h[:].rearrange("p (b t) -> p b t", b=4)
                        carry_b[k] = em.emit("act",
                            (lambda dc=dc, n=n: lambda e: e.activation(
                                out=c_carry[:, dc * NN + n - 3:dc * NN + n + 1],
                                in_=hview[:, :, P - 1:P], func=AF.Copy))(),
                            deps=[hC_t[k]], inc="act_e")

            def emit_ids(dc):
                deps = [A["dfold"][dc]]
                if dc >= 2:
                    deps.append(ym_ev[dc - 2])
                else:
                    deps.append(X["dn_stt_last"])
                xident[dc] = em.emit("pe",
                    (lambda dc=dc: lambda e: e.matmul(ps_ym[dc % 2][:], c_id[:], t_xbf[:, dc, :],
                                                      start=True, stop=False))(),
                    deps=deps, inc="pe_id")
                for n in range(NN):
                    k = dc * NN + n
                    b16 = k % NN
                    id_last[k] = em.emit("pe",
                        (lambda dc=dc, n=n, b16=b16: lambda e: e.matmul(
                            ps_ym[dc % 2][:], c_id[:], t_hC[:, b16 * P:(b16 + 1) * P],
                            start=False, stop=False))(),
                        deps=[hC_t[k]] + ([("dve", hC_t[k][1] + 1, "dve")] if k < DC * NN - 1 else []),
                        inc="pe_id")
                em.emit("pe",
                    (lambda dc=dc: lambda e: e.matmul(ps_ym[dc % 2][:, 0:128], c_zero[:], c_id[:],
                                                      start=False, stop=False))(),
                    deps=[t_z0], inc="pe_id")
                zid = em.emit("pe",
                    (lambda dc=dc: lambda e: e.matmul(ps_ym[dc % 2][:, 0:128], c_zero[:], c_id[:],
                                                      start=False, stop=True))(),
                    inc="pe_id")
                deps = [zid]
                if j > 0 or True:
                    deps.append(X["dn_stt_last"])
                ym_ev[dc] = em.emit("act",
                    (lambda dc=dc: lambda e: e.activation(out=t_ym[:, dc, :], in_=ps_ym[dc % 2][:],
                                                          func=AF.Copy))(),
                    deps=deps, inc="act_e")

            emit_wug_up(0)
            emit_wug_up(1)
            for dc in range(DC):
                for q in range(4):
                    ic = 4 * dc + q
                    emit_up(ic)
                    if ic + 2 < IC:
                        emit_wug_up(ic + 2)
                    if ic >= 1:
                        emit_upev(ic - 1)
                emit_slices(dc)
                if dc >= 1:
                    emit_ids(dc - 1)
            id_total = ("pe_id", em.counts["pe_id"] + 17 + 2, "pe")
            emit_upev(IC - 1, extra=("pe_id", em.counts["pe_id"], "pe"))
            emit_ids(DC - 1)
            us_in = {}

            def emit_gev(ic):
                deps = [g_mm[min(ic + 1, IC - 1)]]
                if ic == IC - 1:
                    deps.append(id_total)
                if ic >= 2:
                    deps.append(("dve_s", base_s + ic - 1, "dve"))
                else:
                    deps.append(X["silu_last"])
                g_ev[ic] = em.emit("act",
                    (lambda ic=ic: lambda e: e.activation(
                        out=t_gs[:, (ic % 2) * P:(ic % 2 + 1) * P], in_=ps_g[ic % 2][:], func=AF.Silu,
                        scale=c_sg[:, ic:ic + 1]))(),
                    deps=deps, inc="act_g")

            def emit_silu(ic):
                deps = [us_out[ic]]
                if ic >= 2:
                    deps.append(silu_t[ic - 2])
                else:
                    deps.append(X["silu_last"])
                us_in[ic] = em.dma("sync",
                    (lambda ic=ic: lambda e: e.dma_start(
                        out=t_us[:, (ic % 2) * P:(ic % 2 + 1) * P], in_=d_us[ic]))(),
                    inc="ld", deps=deps)
                deps = [us_in[ic], g_ev[ic]]
                if ic < IC - 2:
                    deps.append(("act_g", g_ev[ic][1] + 1, "act"))
                if ic >= 2:
                    deps.append(a_out[ic - 2])
                else:
                    deps.append(X["a_out_prev_last"])
                silu_t[ic] = em.emit("dve",
                    (lambda ic=ic: lambda e: e.tensor_mul(
                        out=t_ab[:, (ic % 2) * P:(ic % 2 + 1) * P],
                        in0=t_gs[:, (ic % 2) * P:(ic % 2 + 1) * P],
                        in1=t_us[:, (ic % 2) * P:(ic % 2 + 1) * P]))(),
                    deps=deps, inc="dve_s")
                a_out[ic] = em.dma("sync",
                    (lambda ic=ic: lambda e: e.dma_start(
                        out=d_a[ic], in_=t_ab[:, (ic % 2) * P:(ic % 2 + 1) * P]))(),
                    inc="ld", deps=[silu_t[ic]])

            emit_wug_g(0)
            emit_wug_g(1)
            for ic in range(IC):
                emit_gate(ic)
                if ic + 2 < IC:
                    emit_wug_g(ic + 2)
                if ic >= 1:
                    emit_gev(ic - 1)
                    emit_silu(ic - 1)
            emit_gev(IC - 1)
            emit_silu(IC - 1)
            return dict(up_mm=up_mm, g_mm=g_mm, up_ev=up_ev, g_ev=g_ev, ym_ev=ym_ev,
                        scan=scan_t, hC=hC_t, dBu=dBu_t, carry_b=carry_b, id_last=id_last,
                        silu=silu_t, a_out=a_out, xident=xident)

        def phase_dn(j, FM):
            stt = {}
            outd = {}
            dn_last = {}
            wdn_d, a_in = {}, {}

            def emit_stts(gi):
                oc0, width = DN_GROUPS[gi]
                slack_target = em.counts["pe_dn"]
                def emit_stt(w):
                    oc = oc0 + w
                    deps = [dn_last[gi * 64 + 63], FM["ym_ev"][oc],
                            ("pe_dn", slack_target, "pe")]
                    if oc >= 2:
                        deps.append(outd.get(oc - 2))
                    else:
                        deps.append(X["out_prev"].get(oc + 14))
                    stt[oc] = em.emit("dve",
                        (lambda oc=oc, w=w: lambda e: e.scalar_tensor_tensor(
                            out=t_out[:, (oc % 2) * P:(oc % 2 + 1) * P], in0=ps_dn[w][:],
                            scalar=c_sdp[:, oc:oc + 1], in1=t_ym[:, oc, :],
                            op0=ALU.mult, op1=ALU.add))(),
                        deps=deps, inc="dve")

                def emit_outd(w, slack):
                    oc = oc0 + w
                    outd[oc] = em.dma("sync",
                        (lambda oc=oc, jj=j: lambda e: e.dma_start(
                            out=d_out[:, oc, jj * P:(jj + 1) * P],
                            in_=t_out[:, (oc % 2) * P:(oc % 2 + 1) * P]))(),
                        inc="outd", deps=[stt[oc], slack])

                emit_stt(0)
                for w in range(1, width):
                    emit_stt(w)
                    emit_outd(w - 1, stt[oc0 + w])
                dumm = em.emit("dve", lambda e: e.tensor_copy(out=c_eps[:], in_=c_eps[:]), inc="dve")
                emit_outd(width - 1, dumm)

            for gi, (oc0, width) in enumerate(DN_GROUPS):
                cols0, colsw = oc0 * 128, width * 128
                for kc in range(64):
                    gk = gi * 64 + kc
                    deps = []
                    if j == 0 and gi == 0 and kc < 6:
                        deps.append(fold_done)
                    if gk >= 6:
                        deps.append(dn_last.get(gk - 6))
                    else:
                        deps.append(X["dn_mm_prev_last"])
                    wdn_d[gk] = em.dma("sync",
                        (lambda kc=kc, cols0=cols0, colsw=colsw, b=gk % 6: lambda e: e.dma_start(
                            out=t_wdn[:, b, 0:colsw], in_=d_wd2[kc, :, cols0:cols0 + colsw]))(),
                        inc="ld", deps=deps)
                    if kc % 4 == 0:
                        deps = [FM["a_out"][min(kc + 3, IC - 1)]]
                        if gk >= 8:
                            deps.append(dn_last.get(gk - 5))
                        else:
                            deps.append(X["dn_mm_prev_last"])
                        a_in[(gi, kc // 4)] = em.dma("sync",
                            (lambda kc=kc, s=(kc // 4) % 2: lambda e: e.dma_start(
                                out=t_ain[:, s], in_=d_a[kc:kc + 4].rearrange("k p t -> p k t")))(),
                            inc="ld", deps=deps)
                    mm_deps = [wdn_d[gk], a_in[(gi, kc // 4)]]
                    if kc == 0:
                        mm_deps.append(FM["ym_ev"][DC - 1])
                        if gi > 0:
                            mm_deps.append(stt[DN_GROUPS[gi - 1][0] + DN_GROUPS[gi - 1][1] - 1])
                    for w in range(width):
                        dn_last[gk] = em.emit("pe",
                            (lambda w=w, kc=kc, colsw=colsw, b=gk % 6, s=(kc // 4) % 2, k4=kc % 4: lambda e: e.matmul(
                                ps_dn[w][:], t_wdn[:, b, w * 128:(w + 1) * 128],
                                t_ain[:, s, k4, :],
                                start=(kc == 0), stop=(kc == 63)))(),
                            deps=(mm_deps if w == 0 else []),
                            inc=("pe_dn" if w == width - 1 else None))
                for w4 in range(4):
                    em.emit("pe", lambda e: e.matmul(ps_a[:, 0:128], c_zero[:], c_id[:],
                                                     start=True, stop=True), inc="pe_dn")
                emit_stts(gi)
            return dict(stt=stt, outd=outd, dn_mm_last=dn_last[2 * 64 + 63])

        A = phase_A(0) if nst > 0 else None
        for j in range(nst):
            if not with_fm:
                if j + 1 < nst:
                    A = phase_A(j + 1)
                continue
            FM = phase_fm(j, A)
            X["hC_prev"] = {(k // NN, k % NN): v for k, v in FM["hC"].items()}
            X["dBu_last"] = FM["dBu"][DC * NN - 1]
            X["carry_prev_last"] = FM["carry_b"][DC * NN - 1]
            X["gate_last"] = FM["g_mm"][IC - 1]
            X["g_last2"] = FM["g_mm"][IC - 2]
            X["up_last"] = FM["up_mm"][IC - 1]
            X["up_last2"] = FM["up_mm"][IC - 2]
            X["us_out_last"] = FM["silu"][IC - 1]
            X["idx_last"] = FM["xident"][DC - 1]
            X["id_prev_last"] = FM["id_last"][DC * NN - 1]
            X["up_ev_last"] = FM["up_ev"][IC - 1]
            X["g_ev_last"] = FM["g_ev"][IC - 1]
            X["silu_last"] = FM["silu"][IC - 1]
            X["a_out_prev_last"] = FM["a_out"][IC - 1]
            X["scan_last2"] = {k: FM["scan"][DC * NN - 2 + (k % 2)] for k in range(4)}
            X["last_A_pe"] = A["last_A_pe"]
            X["xh_prev_done"] = A["xh_done"]
            if j + 1 < nst:
                A_next = phase_A(j + 1)
            else:
                A_next = None
            if debug_tap and j == 0:
                k255 = DC * NN - 1
                em.dma("sync", lambda e: e.dma_start(out=d_dbg2[0], in_=t_dA[:, P:2 * P]),
                       inc="ld", deps=[FM["scan"][k255]])
                em.dma("pool", lambda e: e.dma_start(out=d_dbg2[1], in_=t_dBu[:, P:2 * P]),
                       inc="gld", deps=[FM["scan"][k255]])
                em.dma("pool", lambda e: e.dma_start(out=d_dbg2[2], in_=t_h[:, 3 * P:4 * P]),
                       inc="gld", deps=[FM["hC"][k255]])
                em.dma("pool", lambda e: e.dma_start(out=d_dbg2[3], in_=t_hC[:, 3 * P:4 * P]),
                       inc="gld", deps=[FM["id_last"][k255]])
                for dc in range(DC):
                    em.dma("sync", (lambda dc=dc: lambda e: e.dma_start(
                        out=d_dbg[0, :, dc, :], in_=t_ym[:, dc, :]))(),
                        inc="ld", deps=[FM["ym_ev"][dc]])
                    em.dma("sync", (lambda dc=dc: lambda e: e.dma_start(
                        out=d_dbg[1, :, dc, :], in_=t_xn[:, dc, :]))(),
                        inc="ld", deps=[A["xh_done"]])
                    em.dma("sync", (lambda dc=dc: lambda e: e.dma_start(
                        out=d_dbg[2, :, dc, :], in_=t_dl[:, dc, :]))(),
                        inc="ld", deps=[A["dl_ev"][dc]])
                    em.dma("sync", (lambda dc=dc: lambda e: e.dma_start(
                        out=d_dbg[3, :, dc, :], in_=t_xbf[:, dc, :]))(),
                        inc="ld", deps=[A["dfold"][dc]])
            if with_dn:
                DN = phase_dn(j, FM)
                X["dn_out_last"] = DN["outd"][DC - 1]
                X["dn_stt_last"] = DN["stt"][DN_GROUPS[2][0] + DN_GROUPS[2][1] - 1]
                X["dn_mm_prev_last"] = DN["dn_mm_last"]
                X["out_prev"] = DN["outd"]
            if A_next is not None:
                A = A_next

        em.emit("act", lambda e: e.activation(out=t_sq[:, 0:8], in_=t_sq[:, 0:8], func=AF.Copy), inc="act_m")
        em.emit("act", lambda e: e.activation(out=t_sq[:, 0:8], in_=t_sq[:, 0:8], func=AF.Copy), inc="act_g")
        em.emit("act", lambda e: e.activation(out=t_sq[:, 0:8], in_=t_sq[:, 0:8], func=AF.Copy), inc="act_e")
        em.emit("dve", lambda e: e.tensor_copy(out=c_eps[:], in_=c_eps[:]), inc="dve_s")
        em.emit("dve", lambda e: e.tensor_copy(out=c_eps[:], in_=c_eps[:]), inc="dve")
        em.emit("pool", lambda e: e.tensor_copy(out=c_eps[:], in_=c_eps[:]), inc="pool")
        for w in range(4):
            em.emit("pe", lambda e: e.matmul(ps_a[:], c_h128[:], t_f1[:, 0:P], start=True, stop=True),
                    inc="pe_f")

        with contextlib.ExitStack() as sst:
            sems = {n: sst.enter_context(nc.semaphore(n)) for n in em.semnames}
            block = sst.enter_context(nc.Block())

            def runner(eng_name):
                def run(eng):
                    for waits, fn, inc, inc_by in em.streams[eng_name]:
                        for s, v in waits:
                            eng.wait_ge(sems[s], v)
                        inst = fn(eng)
                        if inc is not None and inst is not None:
                            inst.then_inc(sems[inc], inc_by)
                    if eng_name == "sync":
                        eng.wait_ge(sems["ld"], em.counts["ld"])
                        eng.wait_ge(sems["gld"], em.counts["gld"])
                        eng.wait_ge(sems["outd"], em.counts["outd"])
                return run

            block.tensor(runner("pe"))
            block.scalar(runner("act"))
            block.vector(runner("dve"))
            block.gpsimd(runner("pool"))
            block.sync(runner("sync"))

    return nc


def stage_inputs(inputs):
    ih = {}
    x = np.ascontiguousarray(np.asarray(inputs["L0_attn_x"], np.float32))
    A_log = np.asarray(inputs["L0_attn_A_log"], np.float32)
    Dv = np.asarray(inputs["L0_attn_D"], np.float32)
    xpw = np.asarray(inputs["L0_attn_x_proj_w"], np.float32)
    wdt = np.asarray(inputs["L0_attn_dt_proj_w"], np.float32)
    dtb = np.asarray(inputs["L0_attn_dt_proj_b"], np.float32)
    su_ug = np.asarray(inputs["L0_ffn_SU_ug"], np.float32)
    su_dp = np.asarray(inputs["L0_ffn_SU_dp"], np.float32)
    wsc_ug = np.asarray(inputs["L0_ffn_Wscale_ug"], np.float32)
    wsc_dp = np.asarray(inputs["L0_ffn_Wscale_dp"], np.float32)
    wup = np.asarray(inputs["L0_ffn_W_up"], np.float32)
    wg = np.asarray(inputs["L0_ffn_W_gate"], np.float32)
    wd = np.asarray(inputs["L0_ffn_W_down"], np.float32)

    ih["a_log_r"] = A_log.reshape(DC, 128, NN).transpose(1, 0, 2).reshape(128, DC * NN).copy()
    ih["d_r"] = Dv.reshape(DC, 128).T.copy()
    ih["dtb_r"] = dtb.reshape(DC, 128).T.copy()
    ih["wdt_s"] = wdt.reshape(DC, 128, DT_RANK).transpose(2, 0, 1).reshape(128, D).copy()
    ih["xpw_s"] = xpw.reshape(160, DC, 128).transpose(2, 1, 0).reshape(128, DC * 160).copy()
    ih["wsc_up_r"] = wsc_ug[:INTER].reshape(IC, 128).T.copy()
    ih["wsc_g_r"] = wsc_ug[INTER:].reshape(IC, 128).T.copy()
    ih["su_dp_r"] = su_dp.reshape(IC, 128).T.copy()
    ih["wsc_dp_r"] = wsc_dp.reshape(DC, 128).T.copy()
    ih["su_ug_r"] = su_ug.reshape(DC, 128).T.copy()
    ih["wup_s"] = wup.reshape(IC, 128, DC, 128).transpose(0, 3, 2, 1).reshape(IC, 128, D).copy()
    ih["wg_s"] = wg.reshape(IC, 128, DC, 128).transpose(0, 3, 2, 1).reshape(IC, 128, D).copy()
    ih["wd_p1"] = wd.reshape(D, 64, 128).transpose(2, 1, 0).reshape(128, 64 * D).copy()
    ih["h128"] = _had(128).astype(BF)
    ih["h64"] = _had(64).astype(BF)
    ih["ident"] = np.eye(128, dtype=np.float32).astype(BF)
    ih["ones_bf"] = np.ones((128, 1), np.float32).astype(BF)

    in_maps = []
    for b in range(B):
        m = dict(ih)
        m["x_sw"] = x[b].view(np.uint16).reshape(NST, P, DC, 128, 2).transpose(
            0, 2, 1, 4, 3).reshape(NST * DC * P * 2, 128).copy()
        in_maps.append(m)
    return in_maps


def kernel(**inputs):
    nc = build_program()
    in_maps = stage_inputs(inputs)
    res = run_bass_kernel_spmd(nc, in_maps, core_ids=list(range(B)))
    outs = []
    for b in range(B):
        y = np.asarray(res.results[b]["y"])
        outs.append(y.transpose(2, 1, 0).reshape(T, D))
    return np.stack(outs, axis=0)


if __name__ == "__main__":
    import reference as refmod
    inputs = refmod.setup_inputs(0)
    out = kernel(**{k: np.asarray(v) for k, v in inputs.items()})
    ref = np.asarray(refmod.reference(**inputs))
    rel = np.linalg.norm(out - ref) / np.linalg.norm(ref)
    print("rel:", rel)
